# version 02
# baseline (speedup 1.0000x reference)
import numpy as np
from contextlib import ExitStack

import concourse.bass as bass
import concourse.mybir as mybir
from concourse.bass_utils import run_bass_kernel_spmd
from concourse.aot_env import allow_ici_all_to_all

F32 = mybir.dt.float32
BF16 = mybir.dt.bfloat16
AF = mybir.ActivationFunctionType
ALU = mybir.AluOpType
AXX = mybir.AxisListType.X

class _StopEmit(Exception):
    pass


NCORES = 8
L = 16384
DM = 1024
DI = 2048
DS = 16
DC = DI // NCORES
TC = L // NCORES
LC1 = 1024
NCH1 = L // LC1
LC2 = 2048
NCH2 = L // LC2
NT = (DC * DS) // 128
EPS = 1e-5
FFN_E = 8
P3B = 3
TR_LT = 16
RT_LT = 16
WSZ = DM * DI


class _Plan:

    ENGINES = ("sync", "gpsimd", "tensor", "vector", "scalar")

    def __init__(self, nc):
        self.nc = nc
        self.ops = {e: [] for e in self.ENGINES}
        self.sem_h = {}
        self.sem_v = {}

    def make_sems(self, ctx, names):
        for n in names:
            self.sem_h[n] = ctx.enter_context(self.nc.semaphore(n))
            self.sem_v[n] = 0

    AUTO = {"vector": "V", "scalar": "A"}

    def inc(self, eng, sem, fn, n=1):
        if eng in self.AUTO:
            assert sem == self.AUTO[eng]
            return self._auto(eng, fn)
        h = self.sem_h[sem]
        self.sem_v[sem] += n
        v = self.sem_v[sem]

        def _op(e, fn=fn, h=h, n=n):
            fn(e).then_inc(h, n)
        self.ops[eng].append(_op)
        return v

    def _auto(self, eng, fn):
        sem = self.AUTO[eng]
        h = self.sem_h[sem]
        prev = self.sem_v[sem]
        if prev > 0:
            def _w(e, h=h, prev=prev):
                e.wait_ge(h, prev)
            self.ops[eng].append(_w)
        self.sem_v[sem] = prev + 1

        def _op(e, fn=fn, h=h):
            fn(e).then_inc(h, 1)
        self.ops[eng].append(_op)
        return prev + 1

    def run(self, eng, fn):
        if eng in self.AUTO:
            self._auto(eng, fn)
            return
        self.ops[eng].append(fn)

    def wait(self, eng, sem, v):
        if v <= 0:
            return
        h = self.sem_h[sem]

        def _op(e, h=h, v=v):
            e.wait_ge(h, v)
        self.ops[eng].append(_op)

    def val(self, sem):
        return self.sem_v[sem]


def build_program(taps=(), max_phase=4):
    nc = bass.Bass("TRN2", target_bir_lowering=False, debug=False)

    def din(name, shape, dtype=F32):
        return nc.dram_tensor(name, shape, dtype, kind="ExternalInput").ap()

    def dout(name, shape, dtype=F32):
        return nc.dram_tensor(name, shape, dtype, kind="ExternalOutput").ap()

    def dint(name, shape, dtype=F32):
        return nc.dram_tensor(name, shape, dtype).ap()

    xT = din("xT", [DM, L])
    x_res = din("x_res", [TC, DM])
    winc = din("winc", [DM, 2 * DC])
    xpw_pat = din("xpw_pat", [128, 2 * 33])
    dtw_pat = din("dtw_pat", [128, NT])
    dtb_pat = din("dtb_pat", [128, NT])
    dtw_sm = din("dtw_sm", [128, 2])
    dtb_sm = din("dtb_sm", [128, 2])
    alog_pat = din("alog_pat", [128, NT])
    d_pat = din("d_pat", [128, 2])
    convw_pat = din("convw_pat", [128, 8])
    convb_pat = din("convb_pat", [128, 2])
    g_exp = din("g_exp", [128, 16 * 128])
    ident_in = din("ident_in", [128, 128])
    ones_in = din("ones_in", [1, 128])
    wout_in = din("wout_in", [DI, DM])
    wr_pat = din("wr_pat", [128, 64])
    br_in = din("br_in", [1, 8])
    b1_pat = din("b1_pat", [128, 8 * 16])
    b2_in = din("b2_in", [1, 8 * DM])
    w1c = din("w1c", [DM, DI])
    w2c = din("w2c", [DI, DM])

    out = dout("out", [TC, DM])

    xs_d = dint("xs_d", [DC, L], BF16)
    zs_d = dint("zs_d", [DC, L], BF16)
    u_d = dint("u_d", [DC, L], BF16)
    bcd_in = dint("bcd_in", [33, L])
    bcd_out = dint("bcd_out", [33, L])
    a2a_in = dint("a2a_in", [NCORES, DC, TC], BF16)
    a2a_out = dint("a2a_out", [NCORES, DC, TC], BF16)
    wbf_in = dint("wbf_in", [2 * WSZ], BF16)
    wbf_out = dint("wbf_out", [NCORES, 2 * WSZ], BF16)
    x1_d = dint("x1_d", [TC, DM])

    tap_outs = {}
    for t in taps:
        shp, dt = {
            "xs": ([DC, L], BF16), "zs": ([DC, L], BF16), "u": ([DC, L], BF16),
            "bcd": ([33, L], F32), "y1": ([NCORES, DC, TC], BF16),
            "x1": ([TC, DM], F32), "wbf": ([NCORES, 2 * WSZ], BF16),
        }[t]
        tap_outs[t] = dout("tap_" + t, shp, dt)

    with ExitStack() as octx:
        def sbuf(ctx, name, shape, dtype):
            return ctx.enter_context(nc.sbuf_tensor(name, shape, dtype))

        pA = octx.enter_context(nc.psum_tensor("pA", [128, 1024], F32))
        pB = octx.enter_context(nc.psum_tensor("pB", [128, 1024], F32))
        pC = octx.enter_context(nc.psum_tensor("pC", [128, 1024], F32))
        pD = octx.enter_context(nc.psum_tensor("pD", [128, 1024], F32))

        c_dtw = sbuf(octx, "c_dtw", [128, NT], F32)
        c_dtb = sbuf(octx, "c_dtb", [128, NT], F32)
        c_dtws = sbuf(octx, "c_dtws", [128, 2], F32)
        c_dtbs = sbuf(octx, "c_dtbs", [128, 2], F32)
        c_A = sbuf(octx, "c_A", [128, NT], F32)
        c_D = sbuf(octx, "c_D", [128, 2], F32)
        c_cw = sbuf(octx, "c_cw", [128, 8], F32)
        c_cb = sbuf(octx, "c_cb", [128, 2], F32)
        c_id = sbuf(octx, "c_id", [128, 128], F32)
        c_ones = sbuf(octx, "c_ones", [1, 128], BF16)
        c_wr = sbuf(octx, "c_wr", [128, 64], BF16)
        c_br = sbuf(octx, "c_br", [1, 8], BF16)
        c_b1 = sbuf(octx, "c_b1", [128, 8 * 16], F32)
        c_b2 = sbuf(octx, "c_b2", [1, 8 * DM], BF16)
        c_win = sbuf(octx, "c_win", [128, 8 * 512], BF16)
        c_xpw = sbuf(octx, "c_xpw", [128, 2 * 33], BF16)
        carry = sbuf(octx, "carry", [128, NT], F32)
        c_k = sbuf(octx, "c_k", [128, 3], F32)

        P = _Plan(nc)
        P.make_sems(octx, ["boot", "boot2", "xt0", "xt1", "wsc0", "wsc1", "ps0",
                    "ps1", "sp1", "pre", "bmc", "ur0", "ur1", "ys0", "ys1",
                    "y1t", "xres0", "xres1", "x1s0", "x1s1", "w1l", "w2l",
                    "os0", "os1", "T", "V", "A", "CC"])

        for dst, src in [(c_dtw, dtw_pat), (c_dtb, dtb_pat), (c_dtws, dtw_sm),
                         (c_dtbs, dtb_sm), (c_D, d_pat), (c_cw, convw_pat),
                         (c_cb, convb_pat), (c_id, ident_in), (c_b1, b1_pat)]:
            P.inc("sync", "boot", lambda e, d=dst, s=src: e.dma_start(out=d[:, :], in_=s), 16)
        P.inc("sync", "boot", lambda e: e.dma_start(out=c_A[:, :], in_=alog_pat), 16)
        v_const_sld = P.val("boot")
        for dst, src in [(c_ones, ones_in), (c_wr, wr_pat), (c_br, br_in),
                         (c_b2, b2_in), (c_xpw, xpw_pat)]:
            P.inc("gpsimd", "boot2", lambda e, d=dst, s=src: e.dma_start(out=d[:, :], in_=s), 16)
        for kt in range(8):
            P.inc("gpsimd", "boot2",
                  lambda e, kt=kt: e.dma_start(
                      out=c_win[:, kt * 512:(kt + 1) * 512],
                      in_=winc[kt * 128:(kt + 1) * 128, :]), 16)
        v_const_gld = P.val("boot2")
        P.wait("vector", "boot", v_const_sld)
        P.run("vector", lambda e: e.memset(c_k[:, 0:1], EPS))
        P.run("vector", lambda e: e.memset(c_k[:, 1:2], -0.5))
        v_ck = P.inc("vector", "V", lambda e: e.memset(c_k[:, 2:3], -1.0))
        P.wait("scalar", "boot", v_const_sld)
        vAx = P.inc("scalar", "A", lambda e: e.activation(c_A[:, :], c_A[:, :], AF.Exp))
        P.wait("scalar", "A", vAx)
        P.wait("scalar", "V", v_ck)
        v_cA = P.inc("scalar", "A", lambda e: e.mul(c_A[:, :], c_A[:, :], c_k[:, 2:3]))

        try:
            with ExitStack() as sAB:
                s_wout = sbuf(sAB, "s_wout", [128, 16 * 1024], BF16)
                c_g = sbuf(sAB, "c_g", [128, 16 * 128], BF16)
                v_cg = P.inc("gpsimd", "boot2",
                             lambda e: e.dma_start(out=c_g[:, :], in_=g_exp), 16)

                with ExitStack() as sA1:
                    xTb = [sbuf(sA1, f"xTb{i}", [128, 8 * LC1], BF16) for i in range(2)]
                    x_buf = [sbuf(sA1, f"x_buf{i}", [128, 3 + LC1], F32) for i in range(2)]
                    cv = [sbuf(sA1, f"cv{i}", [128, LC1], F32) for i in range(2)]
                    xs_c = [sbuf(sA1, f"xs_c{i}", [128, LC1], BF16) for i in range(2)]
                    zs_c = [sbuf(sA1, f"zs_c{i}", [128, LC1], BF16) for i in range(2)]
                    bcdp = sbuf(sA1, "bcdp", [33, L], F32)
                    wsc = [sbuf(sA1, f"wsc{i}", [128, 2048], BF16) for i in range(2)]

                    P.run("vector", lambda e: e.memset(x_buf[0][:, 0:3], 0.0))
                    P.inc("vector", "V", lambda e: e.memset(x_buf[1][:, 0:3], 0.0))

                    precast = []
                    for i in range(8):
                        precast.append((i * 128 * DI, w1c[i * 128:(i + 1) * 128, :], DI))
                    for i in range(16):
                        precast.append((WSZ + i * 128 * DM,
                                        w2c[i * 128:(i + 1) * 128, :], DM))
                    pre_i = 0
                    v_pre_st = 0
                    v_wsc_st = [0, 0]

                    v_T_ip = {}
                    v_A_ev = {}
                    v_V_conv = {}
                    v_A_xs = {}
                    v_T_xp = {}
                    v_V_bc = {}
                    v_sp = {}
                    v_ld = {}

                    for ch in range(NCH1):
                        cb = ch % 2
                        if ch >= 2:
                            P.wait("gpsimd", "T", v_T_ip[(ch - 2, 3)])
                        for kt in range(8):
                            P.inc("gpsimd", f"xt{cb}",
                                  lambda e, kt=kt, ch=ch, cb=cb: e.dma_start(
                                      out=xTb[cb][:, kt * LC1:(kt + 1) * LC1],
                                      in_=xT[kt * 128:(kt + 1) * 128,
                                             ch * LC1:(ch + 1) * LC1]), 16)
                        v_ld[ch] = P.val(f"xt{cb}")
                        for _ in range(2):
                            if pre_i < len(precast):
                                off, src, cols = precast[pre_i]
                                wb = pre_i % 2
                                if v_wsc_st[wb]:
                                    P.wait("gpsimd", f"ps{wb}", v_wsc_st[wb])
                                vg = P.inc("gpsimd", f"wsc{wb}",
                                           lambda e, wb=wb, src=src, cols=cols:
                                           e.dma_start(out=wsc[wb][:, 0:cols], in_=src), 16)
                                P.wait("sync", f"wsc{wb}", vg)
                                v_pre_st = P.inc(
                                    "sync", f"ps{wb}",
                                    lambda e, wb=wb, off=off, cols=cols: e.dma_start(
                                        out=wbf_in[off:off + 128 * cols]
                                        .rearrange("(r c) -> r c", c=cols),
                                        in_=wsc[wb][:, 0:cols]), 16)
                                v_wsc_st[wb] = P.val(f"ps{wb}")
                                pre_i += 1

                        if ch < 2:
                            P.wait("tensor", "boot2", v_cg)
                        P.wait("tensor", f"xt{cb}", v_ld[ch])
                        for ct in range(4):
                            pp = pA if ct % 2 == 0 else pB
                            if (ch, ct) not in ((0, 0), (0, 1)):
                                prev = (ch, ct - 2) if ct >= 2 else (ch - 1, ct + 2)
                                P.wait("tensor", "A", v_A_ev[prev])
                            for kt in range(8):
                                for nh in range(2):
                                    P.inc("tensor", "T",
                                          lambda e, kt=kt, ct=ct, cb=cb, pp=pp, nh=nh:
                                          e.matmul(
                                              pp[:, nh * 512:(nh + 1) * 512],
                                              c_win[:, kt * 512 + ct * 128:
                                                    kt * 512 + (ct + 1) * 128],
                                              xTb[cb][:, kt * LC1 + nh * 512:
                                                      kt * LC1 + (nh + 1) * 512],
                                              start=(kt == 0), stop=(kt == 7)))
                            v_T_ip[(ch, ct)] = P.val("T")

                            P.wait("scalar", "T", v_T_ip[(ch, ct)])
                            if ct < 2:
                                if ch >= 1:
                                    P.wait("scalar", "V", v_V_conv[(ch - 1, ct)])
                                v_A_ev[(ch, ct)] = P.inc(
                                    "scalar", "A",
                                    lambda e, ct=ct, pp=pp: e.copy(
                                        x_buf[ct][:, 3:3 + LC1], pp[:, :]))
                            else:
                                pt = ct - 2
                                if ch >= 1:
                                    P.wait("scalar", "sp1", v_sp[ch - 1])
                                v_A_ev[(ch, ct)] = P.inc(
                                    "scalar", "A",
                                    lambda e, pt=pt, pp=pp: e.activation(
                                        zs_c[pt][:, :], pp[:, :], AF.Silu))
                        for pt in range(2):
                            P.wait("vector", "A", v_A_ev[(ch, pt)])
                            P.run("vector", lambda e, pt=pt: e.tensor_scalar(
                                cv[pt][:, :], x_buf[pt][:, 0:LC1],
                                c_cw[:, pt * 4:pt * 4 + 1], None, ALU.mult))
                            for k in (1, 2, 3):
                                P.run("vector", lambda e, pt=pt, k=k: e.scalar_tensor_tensor(
                                    cv[pt][:, :], x_buf[pt][:, k:k + LC1],
                                    c_cw[:, pt * 4 + k:pt * 4 + k + 1],
                                    cv[pt][:, :], ALU.mult, ALU.add))
                            v_V_conv[(ch, pt)] = P.inc(
                                "vector", "V",
                                lambda e, pt=pt: e.tensor_copy(
                                    x_buf[pt][:, 0:3], x_buf[pt][:, LC1:LC1 + 3]))
                            P.wait("scalar", "V", v_V_conv[(ch, pt)])
                            if ch >= 1:
                                P.wait("scalar", "sp1", v_sp[ch - 1])
                            v_A_xs[(ch, pt)] = P.inc(
                                "scalar", "A",
                                lambda e, pt=pt: e.activation(
                                    xs_c[pt][:, :], cv[pt][:, :], AF.Silu,
                                    bias=c_cb[:, pt:pt + 1]))
                        P.wait("sync", "A", max(v_A_xs[(ch, 1)], v_A_ev[(ch, 3)]))
                        for pt in range(2):
                            P.inc("sync", "sp1", lambda e, pt=pt, ch=ch: e.dma_start(
                                out=xs_d[pt * 128:(pt + 1) * 128,
                                         ch * LC1:(ch + 1) * LC1],
                                in_=xs_c[pt][:, :]), 16)
                            P.inc("sync", "sp1", lambda e, pt=pt, ch=ch: e.dma_start(
                                out=zs_d[pt * 128:(pt + 1) * 128,
                                         ch * LC1:(ch + 1) * LC1],
                                in_=zs_c[pt][:, :]), 16)
                        v_sp[ch] = P.val("sp1")

                        P.wait("tensor", "A", v_A_xs[(ch, 1)])
                        if ch >= 1:
                            P.wait("tensor", "V", v_V_bc[ch - 1])
                        for kt in range(2):
                            for nh in range(2):
                                P.inc("tensor", "T", lambda e, kt=kt, nh=nh: e.matmul(
                                    pC[0:33, nh * 512:(nh + 1) * 512],
                                    c_xpw[:, kt * 33:(kt + 1) * 33],
                                    xs_c[kt][:, nh * 512:(nh + 1) * 512],
                                    start=(kt == 0), stop=(kt == 1)))
                        v_T_xp[ch] = P.val("T")
                        P.wait("vector", "T", v_T_xp[ch])
                        v_V_bc[ch] = P.inc("vector", "V", lambda e, ch=ch: e.tensor_copy(
                            bcdp[:, ch * LC1:(ch + 1) * LC1], pC[0:33, 0:LC1]))

                    P.wait("gpsimd", "ps0", P.val("ps0"))
                    P.wait("gpsimd", "ps1", P.val("ps1"))
                    v_cc_ag = P.inc("gpsimd", "CC", lambda e: e.collective_compute(
                        "AllGather", ALU.bypass,
                        replica_groups=[list(range(NCORES))],
                        ins=[wbf_in.opt()], outs=[wbf_out.opt()]))

                    P.wait("sync", "V", v_V_bc[NCH1 - 1])
                    v_bcd_sp = P.inc("sync", "sp1",
                                     lambda e: e.dma_start(out=bcd_in, in_=bcdp[:, :]), 16)
                    P.wait("gpsimd", "sp1", v_bcd_sp)
                    v_cc_ar = P.inc("gpsimd", "CC", lambda e: e.collective_compute(
                        "AllReduce", ALU.add,
                        replica_groups=[list(range(NCORES))],
                        ins=[bcd_in.opt()], outs=[bcd_out.opt()]))

                    if "xs" in taps:
                        P.wait("sync", "sp1", P.val("sp1"))
                        P.inc("sync", "os0", lambda e: e.dma_start(
                            out=tap_outs["xs"], in_=xs_d), 16)
                    if "zs" in taps:
                        P.wait("sync", "sp1", P.val("sp1"))
                        P.inc("sync", "os0", lambda e: e.dma_start(
                            out=tap_outs["zs"], in_=zs_d), 16)
                    if "wbf" in taps:
                        P.wait("sync", "CC", v_cc_ag)
                        P.inc("sync", "os0", lambda e: e.dma_start(
                            out=tap_outs["wbf"], in_=wbf_out), 16)

                    v_ph1 = {s: P.val(s) for s in ("T", "V", "A")}

                if max_phase < 2:
                    P.wait("sync", "sp1", P.val("sp1"))
                    P.wait("sync", "CC", P.val("CC"))
                    v_ph2 = v_ph1
                if not (max_phase < 2):
                    with ExitStack() as sA2:
                        s_rep = sbuf(sA2, "s_rep", [128, LC2], F32)
                        bm_rep = sbuf(sA2, "bm_rep", [128, LC2], BF16)
                        cc_rep = sbuf(sA2, "cc_rep", [128, LC2], BF16)
                        xs2 = [sbuf(sA2, f"xs2_{i}", [128, LC2], BF16) for i in range(2)]
                        zs2 = [sbuf(sA2, f"zs2_{i}", [128, LC2], BF16) for i in range(2)]
                        dt2 = [sbuf(sA2, f"dt2_{i}", [128, LC2], F32) for i in range(2)]
                        u2 = [sbuf(sA2, f"u2_{i}", [128, LC2], BF16) for i in range(2)]
                        w_rep = sbuf(sA2, "w_rep", [128, LC2], F32)
                        dA2 = [sbuf(sA2, f"dA2_{i}", [128, LC2], F32) for i in range(2)]
                        u_rep = [sbuf(sA2, f"u_rep{i}", [128, LC2], BF16) for i in range(2)]
                        dbx = sbuf(sA2, "dbx", [128, LC2], BF16)
                        h2 = sbuf(sA2, "h2", [128, LC2], BF16)
                        hc2 = [sbuf(sA2, f"hc2_{i}", [128, LC2], BF16) for i in range(2)]
                        y1b = [sbuf(sA2, f"y1b_{i}", [128, LC2], BF16) for i in range(2)]

                        for eng in _Plan.ENGINES:
                            for s in ("T", "V", "A"):
                                P.wait(eng, s, v_ph1[s])
                        P.wait("sync", "CC", v_cc_ar)
                        P.wait("gpsimd", "CC", v_cc_ar)
                        P.wait("tensor", "boot2", v_cg)
                        P.wait("scalar", "A", v_cA)

                        v_V_tile = {}
                        v_A_tile = {}
                        v_T_tile = {}
                        v_V_y1 = {}
                        v_sld_rep = {}
                        v_y1g = {}

                        for ch in range(NCH2):
                            lo = ch * LC2
                            if ch >= 1:
                                P.wait("sync", "A", v_A_tile[(ch - 1, NT - 1)])
                                P.wait("sync", "V", v_V_y1[(ch - 1, 1)])
                            P.inc("sync", "pre", lambda e, lo=lo: e.dma_start(
                                out=s_rep[:, :],
                                in_=bcd_out[32:33, lo:lo + LC2]
                                .unsqueeze(0).broadcast_to([128, 1, LC2])), 16)
                            for pt in range(2):
                                P.inc("sync", "pre", lambda e, pt=pt, lo=lo: e.dma_start(
                                    out=xs2[pt][:, :],
                                    in_=xs_d[pt * 128:(pt + 1) * 128, lo:lo + LC2]), 16)
                                P.inc("sync", "pre", lambda e, pt=pt, lo=lo: e.dma_start(
                                    out=zs2[pt][:, :],
                                    in_=zs_d[pt * 128:(pt + 1) * 128, lo:lo + LC2]), 16)
                            if ch >= 1:
                                P.wait("gpsimd", "V", v_V_tile[(ch - 1, NT - 1)])
                            P.inc("gpsimd", "bmc", lambda e, lo=lo: e.dma_start(
                                out=bm_rep[:, :],
                                in_=bcd_out[0:16, lo:lo + LC2]
                                .unsqueeze(1).broadcast_to([16, 8, LC2])), 16)
                            vg1 = P.inc("gpsimd", "bmc", lambda e, lo=lo: e.dma_start(
                                out=cc_rep[:, :],
                                in_=bcd_out[16:32, lo:lo + LC2]
                                .unsqueeze(1).broadcast_to([16, 8, LC2])), 16)
                            v_pre_sld = P.val("pre")

                            P.wait("scalar", "pre", v_pre_sld)
                            P.run("scalar", lambda e: e.activation(
                                dt2[0][:, :], s_rep[:, :], AF.Exp,
                                bias=c_dtbs[:, 0:1], scale=c_dtws[:, 0:1]))
                            vAx = P.inc("scalar", "A", lambda e: e.activation(
                                dt2[1][:, :], s_rep[:, :], AF.Exp,
                                bias=c_dtbs[:, 1:2], scale=c_dtws[:, 1:2]))
                            P.wait("scalar", "A", vAx)
                            P.run("scalar", lambda e: e.activation(
                                dt2[0][:, :], dt2[0][:, :], AF.Ln, bias=1.0))
                            v_A_dt = P.inc("scalar", "A", lambda e: e.activation(
                                dt2[1][:, :], dt2[1][:, :], AF.Ln, bias=1.0))
                            P.wait("vector", "A", v_A_dt)
                            for pt in range(2):
                                P.run("vector", lambda e, pt=pt: e.tensor_tensor(
                                    u2[pt][:, :], dt2[pt][:, :], xs2[pt][:, :], ALU.mult))
                            v_V_u = P.inc("vector", "V", lambda e: e.nop())
                            P.wait("sync", "V", v_V_u)
                            for pt in range(2):
                                P.inc("sync", "pre", lambda e, pt=pt, lo=lo: e.dma_start(
                                    out=u_d[pt * 128:(pt + 1) * 128, lo:lo + LC2],
                                    in_=u2[pt][:, :]), 16)
                            v_u_sp = P.val("pre")

                            for t in range(NT):
                                tb = t % 2
                                g = t // 16
                                m = t % 16
                                if (ch, t) not in ((0, 0), (0, 1)):
                                    pch, pt_ = (ch, t - 2) if t >= 2 else (ch - 1, NT - 2 + t)
                                    P.wait("sync", "V", v_V_tile[(pch, pt_)])
                                if t == 0:
                                    P.wait("sync", "pre", v_u_sp)
                                v_sld_rep[(ch, t)] = P.inc(
                                    "sync", f"ur{tb}", lambda e, t=t, lo=lo, tb=tb: e.dma_start(
                                        out=u_rep[tb][:, :],
                                        in_=u_d[8 * t:8 * t + 8, lo:lo + LC2]
                                        .unsqueeze(0).broadcast_to([16, 8, LC2])), 16)
                                if (ch, t) not in ((0, 0), (0, 1)):
                                    pch, pt_ = (ch, t - 2) if t >= 2 else (ch - 1, NT - 2 + t)
                                    P.wait("scalar", "V", v_V_tile[(pch, pt_)])
                                vAx = P.inc("scalar", "A", lambda e, t=t: e.activation(
                                    w_rep[:, :], s_rep[:, :], AF.Exp,
                                    bias=c_dtb[:, t:t + 1], scale=c_dtw[:, t:t + 1]))
                                P.wait("scalar", "A", vAx)
                                vAx = P.inc("scalar", "A", lambda e: e.activation(
                                    w_rep[:, :], w_rep[:, :], AF.Ln, bias=1.0))
                                P.wait("scalar", "A", vAx)
                                v_A_tile[(ch, t)] = P.inc(
                                    "scalar", "A", lambda e, t=t, tb=tb: e.activation(
                                        dA2[tb][:, :], w_rep[:, :], AF.Exp,
                                        scale=c_A[:, t:t + 1]))
                                P.wait("vector", "A", v_A_tile[(ch, t)])
                                P.wait("vector", f"ur{tb}", v_sld_rep[(ch, t)])
                                if t == 0:
                                    P.wait("vector", "bmc", vg1)
                                if (ch, t) not in ((0, 0), (0, 1)):
                                    pch, pt_ = (ch, t - 2) if t >= 2 else (ch - 1, NT - 2 + t)
                                    P.wait("vector", "T", v_T_tile[(pch, pt_)])
                                P.run("vector", lambda e, tb=tb: e.tensor_tensor(
                                    dbx[:, :], u_rep[tb][:, :], bm_rep[:, :], ALU.mult))
                                if ch == 0:
                                    P.run("vector", lambda e, tb=tb: e.tensor_tensor_scan(
                                        h2[:, :], dA2[tb][:, :], dbx[:, :], 0.0,
                                        ALU.mult, ALU.add))
                                else:
                                    P.run("vector", lambda e, tb=tb, t=t: e.tensor_tensor_scan(
                                        h2[:, :], dA2[tb][:, :], dbx[:, :],
                                        carry[:, t:t + 1], ALU.mult, ALU.add))
                                P.run("vector", lambda e, t=t: e.tensor_copy(
                                    carry[:, t:t + 1], h2[:, LC2 - 1:LC2]))
                                v_V_tile[(ch, t)] = P.inc(
                                    "vector", "V", lambda e, tb=tb: e.tensor_tensor(
                                        hc2[tb][:, :], h2[:, :], cc_rep[:, :], ALU.mult))
                                py_lo, py_hi = (pA, pB) if g == 0 else (pC, pD)
                                P.wait("tensor", "V", v_V_tile[(ch, t)])
                                if m == 0 and ch >= 1:
                                    P.wait("tensor", "V", v_V_y1[(ch - 1, g)])
                                for q in range(3):
                                    py = py_lo if q < 2 else py_hi
                                    P.inc("tensor", "T", lambda e, m=m, tb=tb, py=py, q=q: e.matmul(
                                        py[:, (q % 2) * 512:(q % 2 + 1) * 512],
                                        c_g[:, m * 128:(m + 1) * 128],
                                        hc2[tb][:, q * 512:(q + 1) * 512],
                                        start=(m == 0), stop=(m == 15)))
                                v_T_tile[(ch, t)] = P.inc(
                                    "tensor", "T",
                                    lambda e, m=m, tb=tb, py=py_hi: e.matmul(
                                        py[:, 512:1024], c_g[:, m * 128:(m + 1) * 128],
                                        hc2[tb][:, 1536:2048],
                                        start=(m == 0), stop=(m == 15)))
                                if m == 15:
                                    P.wait("vector", "T", v_T_tile[(ch, t)])
                                    if ch >= 1:
                                        P.wait("vector", f"ys{g}", v_y1g[(ch - 1, g)])
                                    for hf, py in ((0, py_lo), (1, py_hi)):
                                        P.run("vector",
                                              lambda e, g=g, py=py, hf=hf:
                                              e.scalar_tensor_tensor(
                                                  dt2[g][:, hf * 1024:(hf + 1) * 1024],
                                                  xs2[g][:, hf * 1024:(hf + 1) * 1024],
                                                  c_D[:, g:g + 1],
                                                  py[:, :],
                                                  ALU.mult, ALU.add))
                                    v_V_y1[(ch, g)] = P.inc(
                                        "vector", "V", lambda e, g=g: e.tensor_tensor(
                                            y1b[g][:, :], dt2[g][:, :], zs2[g][:, :],
                                            ALU.mult))
                                    P.wait("sync", "V", v_V_y1[(ch, g)])
                                    v_y1g[(ch, g)] = P.inc(
                                        "sync", f"ys{g}",
                                        lambda e, g=g, ch=ch: e.dma_start(
                                            out=a2a_in[ch, g * 128:(g + 1) * 128, :],
                                            in_=y1b[g][:, :]), 16)

                        P.wait("gpsimd", "ys0", v_y1g[(NCH2 - 1, 0)])
                        P.wait("gpsimd", "ys1", v_y1g[(NCH2 - 1, 1)])

                        def _a2a(e):
                            with allow_ici_all_to_all():
                                return e.collective_compute(
                                    "AllToAll", ALU.bypass,
                                    replica_groups=[list(range(NCORES))],
                                    ins=[a2a_in.opt()], outs=[a2a_out.opt()])
                        v_cc_a2a = P.inc("gpsimd", "CC", _a2a)

                        if "u" in taps:
                            P.wait("sync", "pre", v_u_sp)
                            P.inc("sync", "os0", lambda e: e.dma_start(
                                out=tap_outs["u"], in_=u_d), 16)
                        if "bcd" in taps:
                            P.wait("sync", "CC", v_cc_ar)
                            P.inc("sync", "os0", lambda e: e.dma_start(
                                out=tap_outs["bcd"], in_=bcd_out), 16)
                        if "y1" in taps:
                            P.wait("sync", "CC", v_cc_a2a)
                            P.inc("sync", "os0", lambda e: e.dma_start(
                                out=tap_outs["y1"], in_=a2a_out), 16)

                        v_ph2 = {s: P.val(s) for s in ("T", "V", "A")}

                for kt in range(16):
                    P.inc("gpsimd", "boot2", lambda e, kt=kt: e.dma_start(
                        out=s_wout[:, kt * 1024:(kt + 1) * 1024],
                        in_=wout_in[kt * 128:(kt + 1) * 128, :]), 16)
                v_wout = P.val("boot2")

                if max_phase < 3:
                    P.wait("sync", "ys0", P.val("ys0"))
                    P.wait("sync", "ys1", P.val("ys1"))
                    P.wait("sync", "CC", P.val("CC"))
                    v_ph3a = v_ph2
                if not (max_phase < 3):
                    with ExitStack() as s3a:
                        y1T = sbuf(s3a, "y1T", [128, 16 * 1024], BF16)
                        xrb = [sbuf(s3a, f"xrb{i}", [128, 1024], F32) for i in range(2)]
                        xr = [sbuf(s3a, f"xr{i}", [128, 1024], F32) for i in range(2)]
                        sq = sbuf(s3a, "sq", [128, 1024], F32)
                        x1t = [sbuf(s3a, f"x1t{i}", [128, 1024], F32) for i in range(2)]
                        lnA = sbuf(s3a, "lnA", [128, 24], F32)

                        for eng in _Plan.ENGINES:
                            for s in ("T", "V", "A"):
                                P.wait(eng, s, v_ph2[s])
                        P.wait("sync", "CC", v_cc_a2a)
                        P.wait("tensor", "boot2", v_wout)

                        v_T_op = {}
                        v_A_x1 = {}
                        v_x1_sp = {}
                        v_V_fin = {}
                        v_V_add = {}

                        for half in range(2):
                            if half == 1:
                                P.wait("sync", "T", v_T_op[(0, 7)])
                            for kt in range(16):
                                P.inc("sync", "y1t",
                                      lambda e, kt=kt, half=half: e.dma_start(
                                          out=y1T[:, kt * 1024:(kt + 1) * 1024],
                                          in_=a2a_out[kt // 2,
                                                      (kt % 2) * 128:(kt % 2 + 1) * 128,
                                                      half * 1024:(half + 1) * 1024]), 16)
                            v_y1T = P.val("y1t")
                            P.wait("tensor", "y1t", v_y1T)
                            for lt in range(8):
                                tok0 = half * 1024 + lt * 128
                                lb = lt % 2
                                o = lb * 12
                                pp = pA if lb == 0 else pB
                                if (half, lt) not in ((0, 0), (0, 1)):
                                    ph, pl = (half, lt - 2) if lt >= 2 else (0, 6 + lt)
                                    P.wait("tensor", "V", v_V_fin[(ph, pl)])
                                for kt in range(16):
                                    for nh in range(2):
                                        P.inc("tensor", "T",
                                              lambda e, kt=kt, lt=lt, pp=pp, nh=nh:
                                              e.matmul(
                                                  pp[:, nh * 512:(nh + 1) * 512],
                                                  y1T[:, kt * 1024 + lt * 128:
                                                      kt * 1024 + (lt + 1) * 128],
                                                  s_wout[:, kt * 1024 + nh * 512:
                                                         kt * 1024 + (nh + 1) * 512],
                                                  start=(kt == 0), stop=(kt == 15)))
                                v_T_op[(half, lt)] = P.val("T")
                                if (half, lt) not in ((0, 0), (0, 1)):
                                    ph, pl = (half, lt - 2) if lt >= 2 else (0, 6 + lt)
                                    P.wait("sync", "V", v_V_add[(ph, pl)])
                                P.inc("sync", f"xres{lb}",
                                      lambda e, tok0=tok0, lb=lb: e.dma_start(
                                          out=xrb[lb][:, :],
                                          in_=x_res[tok0:tok0 + 128, :]), 16)
                                v_xres = P.val(f"xres{lb}")
                                P.wait("vector", "T", v_T_op[(half, lt)])
                                P.wait("vector", f"xres{lb}", v_xres)
                                if (half, lt) not in ((0, 0), (0, 1)):
                                    ph, pl = (half, lt - 2) if lt >= 2 else (0, 6 + lt)
                                    P.wait("vector", "A", v_A_x1[(ph, pl)])
                                P.run("vector", lambda e, lb=lb, pp=pp: e.tensor_add(
                                    xr[lb][:, :], pp[:, :], xrb[lb][:, :]))
                                v_V_add[(half, lt)] = P.inc("vector", "V",
                                                            lambda e: e.nop())
                                P.run("vector", lambda e, lb=lb, o=o: e.reduce_sum(
                                    lnA[:, o:o + 1], xr[lb][:, :], axis=AXX))
                                P.run("vector", lambda e, lb=lb: e.tensor_tensor(
                                    sq[:, :], xr[lb][:, :], xr[lb][:, :], ALU.mult))
                                v_Vs = P.inc("vector", "V", lambda e: e.nop())
                                P.wait("vector", "V", v_Vs)
                                P.run("vector", lambda e, o=o: e.reduce_sum(
                                    lnA[:, o + 1:o + 2], sq[:, :], axis=AXX))
                                v_Vs = P.inc("vector", "V", lambda e: e.nop())
                                P.wait("vector", "V", v_Vs)
                                P.run("vector", lambda e, o=o: e.tensor_scalar(
                                    lnA[:, o + 2:o + 3], lnA[:, o:o + 1],
                                    1.0 / DM, None, ALU.mult))
                                P.run("vector", lambda e, o=o: e.tensor_scalar(
                                    lnA[:, o + 3:o + 4], lnA[:, o + 1:o + 2],
                                    1.0 / DM, None, ALU.mult))
                                v_Vs = P.inc("vector", "V", lambda e: e.nop())
                                P.wait("vector", "V", v_Vs)
                                P.run("vector", lambda e, o=o: e.tensor_tensor(
                                    lnA[:, o + 4:o + 5], lnA[:, o + 2:o + 3],
                                    lnA[:, o + 2:o + 3], ALU.mult))
                                v_Vs = P.inc("vector", "V", lambda e: e.nop())
                                P.wait("vector", "V", v_Vs)
                                P.run("vector", lambda e, o=o: e.tensor_tensor(
                                    lnA[:, o + 5:o + 6], lnA[:, o + 3:o + 4],
                                    lnA[:, o + 4:o + 5], ALU.subtract))
                                v_Vs = P.inc("vector", "V", lambda e: e.nop())
                                P.wait("vector", "V", v_Vs)
                                v_Vvar = P.inc("vector", "V", lambda e, o=o: e.tensor_scalar(
                                    lnA[:, o + 5:o + 6], lnA[:, o + 5:o + 6],
                                    EPS, None, ALU.add))
                                P.wait("scalar", "V", v_Vvar)
                                vAx = P.inc("scalar", "A", lambda e, o=o: e.activation(
                                    lnA[:, o + 6:o + 7], lnA[:, o + 5:o + 6],
                                    AF.Ln))
                                P.wait("scalar", "A", vAx)
                                v_Ainv = P.inc("scalar", "A", lambda e, o=o: e.activation(
                                    lnA[:, o + 7:o + 8], lnA[:, o + 6:o + 7],
                                    AF.Exp, scale=c_k[:, 1:2]))
                                P.wait("vector", "A", v_Ainv)
                                P.run("vector", lambda e, o=o: e.tensor_tensor(
                                    lnA[:, o + 8:o + 9], lnA[:, o + 2:o + 3],
                                    lnA[:, o + 7:o + 8], ALU.mult))
                                v_Vs = P.inc("vector", "V", lambda e: e.nop())
                                P.wait("vector", "V", v_Vs)
                                v_Vnb = P.inc("vector", "V", lambda e, o=o: e.tensor_scalar(
                                    lnA[:, o + 9:o + 10], lnA[:, o + 8:o + 9],
                                    -1.0, None, ALU.mult))
                                v_V_fin[(half, lt)] = v_Vnb
                                P.wait("scalar", "V", v_Vnb)
                                P.wait("scalar", "A", v_Ainv)
                                if (half, lt) not in ((0, 0), (0, 1)):
                                    ph, pl = (half, lt - 2) if lt >= 2 else (0, 6 + lt)
                                    P.wait("scalar", f"x1s{lb}", v_x1_sp[(ph, pl)])
                                v_A_x1[(half, lt)] = P.inc(
                                    "scalar", "A", lambda e, lb=lb, o=o: e.activation(
                                        x1t[lb][:, :], xr[lb][:, :], AF.Identity,
                                        bias=lnA[:, o + 9:o + 10],
                                        scale=lnA[:, o + 7:o + 8]))
                                P.wait("sync", "A", v_A_x1[(half, lt)])
                                P.inc("sync", f"x1s{lb}",
                                      lambda e, tok0=tok0, lb=lb: e.dma_start(
                                          out=x1_d[tok0:tok0 + 128, :],
                                          in_=x1t[lb][:, :]), 16)
                                v_x1_sp[(half, lt)] = P.val(f"x1s{lb}")

                        if "x1" in taps:
                            P.wait("sync", "x1s0", P.val("x1s0"))
                            P.wait("sync", "x1s1", P.val("x1s1"))
                            P.inc("sync", "os0", lambda e: e.dma_start(
                                out=tap_outs["x1"], in_=x1_d), 16)

                        v_ph3a = {s: P.val(s) for s in ("T", "V", "A")}

            if max_phase < 4:
                P.wait("sync", "x1s0", P.val("x1s0"))
                P.wait("sync", "x1s1", P.val("x1s1"))
            if not (max_phase < 4):
                with ExitStack() as sF:
                    xfT = sbuf(sF, "xfT", [128, 8 * TC], BF16)
                    x1r = [sbuf(sF, f"x1r{i}", [128, 1024], F32) for i in range(2)]
                    gates = sbuf(sF, "gates", [128, 16 * 8], F32)
                    gl2 = sbuf(sF, "gl2", [128, 128], F32)
                    rtc = sbuf(sF, "rtc", [128, 64], F32)
                    acc = sbuf(sF, "acc", [128, 8 * 1024], F32)
                    hbuf = sbuf(sF, "hbuf", [128, 16 * 1024], BF16)
                    w1e = sbuf(sF, "w1e", [128, 8 * 2048], BF16)
                    w2e = sbuf(sF, "w2e", [128, 16 * 1024], BF16)
                    sq2 = sbuf(sF, "sq2", [128, 1024], F32)
                    lnB = sbuf(sF, "lnB", [128, 24], F32)
                    ot = [sbuf(sF, f"ot{i}", [128, 1024], F32) for i in range(2)]

                    for eng in _Plan.ENGINES:
                        for s in ("T", "V", "A"):
                            P.wait(eng, s, v_ph3a[s])
                    P.wait("sync", "x1s0", P.val("x1s0"))
                    P.wait("sync", "x1s1", P.val("x1s1"))

                    v_T_tr = {}
                    v_A_tr = {}
                    for lt in range(TR_LT):
                        lb = lt % 2
                        if lt >= 2:
                            P.wait("sync", "A", v_A_tr[lt - 2])
                        P.inc("sync", f"xres{lb}", lambda e, lt=lt, lb=lb: e.dma_start(
                            out=x1r[lb][:, :], in_=x1_d[lt * 128:(lt + 1) * 128, :]), 16)
                        v_x1r = P.val(f"xres{lb}")
                        P.wait("tensor", f"xres{lb}", v_x1r)
                        if lt >= 2:
                            P.wait("tensor", "A", v_A_tr[lt - 2])
                        pt_ps = pD if lb == 0 else pC
                        for j in range(8):
                            v_T_tr[lt] = P.inc("tensor", "T",
                                               lambda e, j=j, lb=lb, pt_ps=pt_ps: e.transpose(
                                                   pt_ps[:, j * 128:(j + 1) * 128],
                                                   x1r[lb][:, j * 128:(j + 1) * 128],
                                                   c_id[:, :]))
                        P.wait("scalar", "T", v_T_tr[lt])
                        v_A_tr[lt] = P.inc(
                            "scalar", "A",
                            lambda e, lt=lt, pt_ps=pt_ps: e.copy(
                                xfT[:, :].rearrange("p (j tc) -> p j tc", j=8)
                                [:, :, lt * 128:(lt + 1) * 128],
                                pt_ps[:, :].rearrange("p (j f) -> p j f", j=8)))

                    v_T_rt = 0
                    for lt in range(16 if P3B >= 1 else 0):
                        ppo = lt * 8
                        P.wait("tensor", "A", v_A_tr[15])
                        for kt in range(8):
                            P.inc("tensor", "T",
                                  lambda e, kt=kt, lt=lt, ppo=ppo: e.matmul(
                                      pC[:, ppo:ppo + 8],
                                      xfT[:, kt * TC + lt * 128:
                                          kt * TC + (lt + 1) * 128],
                                      c_wr[:, kt * 8:(kt + 1) * 8],
                                      start=(kt == 0), stop=False))
                        v_T_rt = P.inc("tensor", "T",
                                       lambda e, ppo=ppo: e.matmul(
                                           pC[:, ppo:ppo + 8],
                                           c_ones[:1, :], c_br[:1, :],
                                           start=False, stop=True))
                    if P3B >= 1:
                        r3 = lambda ap: ap.rearrange("p (l e) -> p l e", e=8)
                        bc = lambda col: rtc[:, col:col + 16].unsqueeze(2)\
                            .broadcast_to([128, 16, 8])
                        P.wait("vector", "T", v_T_rt)
                        P.run("vector", lambda e: e.tensor_copy(
                            sq2[:, 0:128], pC[:, 0:128]))
                        P.run("vector", lambda e: e.reduce_max(
                            rtc[:, 0:16].unsqueeze(2), r3(sq2[:, 0:128]),
                            axis=AXX))
                        P.run("vector", lambda e: e.tensor_tensor(
                            r3(gates[:, :]), r3(sq2[:, 0:128]), bc(0),
                            ALU.is_equal))
                        P.run("vector", lambda e: e.scalar_tensor_tensor(
                            gl2[:, :], gates[:, :], -1e30, sq2[:, 0:128],
                            ALU.mult, ALU.add))
                        P.run("vector", lambda e: e.reduce_max(
                            rtc[:, 16:32].unsqueeze(2), r3(gl2[:, :]),
                            axis=AXX))
                        v_Vm2 = P.inc("vector", "V", lambda e: e.tensor_tensor(
                            rtc[:, 32:48], rtc[:, 16:32], rtc[:, 0:16],
                            ALU.subtract))
                        P.wait("scalar", "V", v_Vm2)
                        v_Ae2 = P.inc("scalar", "A", lambda e: e.activation(
                            rtc[:, 32:48], rtc[:, 32:48], AF.Exp))
                        P.wait("vector", "A", v_Ae2)
                        P.run("vector", lambda e: e.tensor_scalar(
                            rtc[:, 48:64], rtc[:, 32:48], 1.0, None, ALU.add))
                        P.run("vector", lambda e: e.reciprocal(
                            rtc[:, 48:64], rtc[:, 48:64]))
                        P.run("vector", lambda e: e.tensor_tensor(
                            rtc[:, 32:48], rtc[:, 32:48], rtc[:, 48:64],
                            ALU.mult))
                        P.run("vector", lambda e: e.tensor_tensor(
                            r3(gates[:, :]), r3(gates[:, :]), bc(48), ALU.mult))
                        P.run("vector", lambda e: e.tensor_tensor(
                            r3(gl2[:, :]), r3(gl2[:, :]), bc(16),
                            ALU.is_equal))
                        P.run("vector", lambda e: e.tensor_tensor(
                            r3(gl2[:, :]), r3(gl2[:, :]), bc(32), ALU.mult))
                        P.run("vector", lambda e: e.tensor_tensor(
                            gates[:, :], gl2[:, :], gates[:, :], ALU.add))
                    v_router_done = P.val("V")

                    def load_w1(e_idx, after_T):
                        if after_T:
                            P.wait("sync", "T", after_T)
                        for kt in range(8):
                            P.inc("sync", "w1l", lambda e, e_idx=e_idx, kt=kt: e.dma_start(
                                out=w1e[:, kt * 2048:(kt + 1) * 2048],
                                in_=wbf_out[e_idx, kt * 128 * 2048:(kt + 1) * 128 * 2048]
                                .rearrange("(r c) -> r c", c=2048)), 16)
                        return P.val("w1l")

                    def load_w2(e_idx, after_T):
                        if after_T:
                            P.wait("sync", "T", after_T)
                        for ft in range(16):
                            P.inc("sync", "w2l", lambda e, e_idx=e_idx, ft=ft: e.dma_start(
                                out=w2e[:, ft * 1024:(ft + 1) * 1024],
                                in_=wbf_out[e_idx,
                                            WSZ + ft * 128 * 1024:
                                            WSZ + (ft + 1) * 128 * 1024]
                                .rearrange("(r c) -> r c", c=1024)), 16)
                        return P.val("w2l")

                    if P3B >= 2:
                        P.wait("sync", "CC", v_cc_ag)
                        v_w1ld = {0: load_w1(0, 0)}
                        v_w2ld = {0: load_w2(0, 0)}
                    else:
                        v_w1ld = {}
                        v_w2ld = {}
                    v_A_h = {}
                    v_V_acc = {}
                    v_T_mm1_last = {}
                    v_T_mm2_last = {}
                    v_out_st = {}
                    v_ln2_done = {}

                    for half in range(2 if P3B >= 2 else 0):
                        tk0 = half * 1024
                        for ei in range(FFN_E):
                            seq = half * FFN_E + ei
                            nxt = seq + 1
                            ne = (ei + 1) % FFN_E
                            P.wait("tensor", "w1l", v_w1ld[seq])
                            if seq == 0:
                                P.wait("tensor", "A", v_A_tr[15])
                                P.wait("tensor", "V", v_router_done)
                            if seq >= 1:
                                P.wait("scalar", "T", v_T_mm2_last[seq - 1])
                            for ft in range(16):
                                pp = pC if ft % 2 == 0 else pD
                                if ft >= 2:
                                    P.wait("tensor", "A", v_A_h[ft - 2])
                                elif seq > 0:
                                    P.wait("tensor", "A", v_A_h[14 + ft])
                                for kt in range(8):
                                    for nh in range(2):
                                        v_T_h = P.inc(
                                            "tensor", "T",
                                            lambda e, kt=kt, ft=ft, tk0=tk0, pp=pp, nh=nh:
                                            e.matmul(
                                                pp[:, nh * 512:(nh + 1) * 512],
                                                w1e[:, kt * 2048 + ft * 128:
                                                    kt * 2048 + (ft + 1) * 128],
                                                xfT[:, kt * TC + tk0 + nh * 512:
                                                    kt * TC + tk0 + (nh + 1) * 512],
                                                start=(kt == 0), stop=(kt == 7)))
                                if ft == 15:
                                    v_T_mm1_last[seq] = v_T_h
                                P.wait("scalar", "T", v_T_h)
                                v_A_h[ft] = P.inc(
                                    "scalar", "A", lambda e, ft=ft, ei=ei, pp=pp:
                                    e.activation(
                                        hbuf[:, ft * 1024:(ft + 1) * 1024], pp[:, :],
                                        AF.Relu,
                                        bias=c_b1[:, ei * 16 + ft:ei * 16 + ft + 1]))
                            if nxt < 16:
                                v_w1ld[nxt] = load_w1(ne, v_T_mm1_last[seq])
                            P.wait("tensor", "w2l", v_w2ld[seq])
                            for tt in range(8):
                                glt = half * 8 + tt
                                pp = pA if tt % 2 == 0 else pB
                                P.wait("tensor", "A", v_A_h[15])
                                if (seq, tt) not in ((0, 0), (0, 1)):
                                    ps, pt_ = (seq, tt - 2) if tt >= 2 else (seq - 1, 6 + tt)
                                    P.wait("tensor", "V", v_V_acc[(ps, pt_)])
                                for ft in range(16):
                                    for nh in range(2):
                                        P.inc("tensor", "T",
                                              lambda e, ft=ft, tt=tt, pp=pp, nh=nh:
                                              e.matmul(
                                                  pp[:, nh * 512:(nh + 1) * 512],
                                                  hbuf[:, ft * 1024 + tt * 128:
                                                       ft * 1024 + (tt + 1) * 128],
                                                  w2e[:, ft * 1024 + nh * 512:
                                                      ft * 1024 + (nh + 1) * 512],
                                                  start=(ft == 0), stop=False))
                                P.inc("tensor", "T",
                                      lambda e, ei=ei, pp=pp: e.matmul(
                                          pp[:, 0:512], c_ones[:1, :],
                                          c_b2[:1, ei * 1024:ei * 1024 + 512],
                                          start=False, stop=True))
                                v_T_mm2 = P.inc("tensor", "T",
                                                lambda e, ei=ei, pp=pp: e.matmul(
                                                    pp[:, 512:1024], c_ones[:1, :],
                                                    c_b2[:1, ei * 1024 + 512:
                                                         (ei + 1) * 1024],
                                                    start=False, stop=True))
                                if tt == 7:
                                    v_T_mm2_last[seq] = v_T_mm2
                                P.wait("vector", "T", v_T_mm2)
                                if ei == 0:
                                    if half == 1 and P3B >= 3:
                                        P.wait("vector", "A", v_ln2_done[tt])
                                    v_V_acc[(seq, tt)] = P.inc(
                                        "vector", "V",
                                        lambda e, tt=tt, glt=glt, ei=ei, pp=pp:
                                        e.tensor_scalar(
                                            acc[:, tt * 1024:(tt + 1) * 1024], pp[:, :],
                                            gates[:, glt * 8 + ei:glt * 8 + ei + 1],
                                            None, ALU.mult))
                                else:
                                    v_V_acc[(seq, tt)] = P.inc(
                                        "vector", "V",
                                        lambda e, tt=tt, glt=glt, ei=ei, pp=pp:
                                        e.scalar_tensor_tensor(
                                            acc[:, tt * 1024:(tt + 1) * 1024], pp[:, :],
                                            gates[:, glt * 8 + ei:glt * 8 + ei + 1],
                                            acc[:, tt * 1024:(tt + 1) * 1024],
                                            ALU.mult, ALU.add))
                            if nxt < 16:
                                v_w2ld[nxt] = load_w2(ne, v_T_mm2_last[seq])

                        for tt in range(8 if P3B >= 3 else 0):
                            glt = half * 8 + tt
                            tok0 = glt * 128
                            lb = tt % 2
                            o = lb * 12
                            seq_last = half * FFN_E + FFN_E - 1
                            P.wait("sync", "V", v_V_acc[(seq_last, tt)])
                            if glt >= 2:
                                P.wait("sync", "A", v_ln2_done[glt - 2])
                            P.inc("sync", f"xres{lb}", lambda e, tok0=tok0, lb=lb: e.dma_start(
                                out=x1r[lb][:, :], in_=x1_d[tok0:tok0 + 128, :]), 16)
                            v_x1rr = P.val(f"xres{lb}")
                            P.wait("vector", f"xres{lb}", v_x1rr)
                            if glt >= 2:
                                P.wait("vector", f"os{lb}", v_out_st[glt - 2])
                            P.run("vector", lambda e, tt=tt, lb=lb: e.tensor_add(
                                sq2[:, :], acc[:, tt * 1024:(tt + 1) * 1024],
                                x1r[lb][:, :]))
                            v_Vs = P.inc("vector", "V", lambda e: e.nop())
                            P.wait("vector", "V", v_Vs)
                            P.run("vector", lambda e, o=o: e.reduce_sum(
                                lnB[:, o:o + 1], sq2[:, :], axis=AXX))
                            P.run("vector", lambda e, lb=lb: e.tensor_tensor(
                                ot[lb][:, :], sq2[:, :], sq2[:, :], ALU.mult))
                            v_Vs = P.inc("vector", "V", lambda e: e.nop())
                            P.wait("vector", "V", v_Vs)
                            P.run("vector", lambda e, lb=lb, o=o: e.reduce_sum(
                                lnB[:, o + 1:o + 2], ot[lb][:, :], axis=AXX))
                            v_Vs = P.inc("vector", "V", lambda e: e.nop())
                            P.wait("vector", "V", v_Vs)
                            P.run("vector", lambda e, o=o: e.tensor_scalar(
                                lnB[:, o + 2:o + 3], lnB[:, o:o + 1],
                                1.0 / DM, None, ALU.mult))
                            P.run("vector", lambda e, o=o: e.tensor_scalar(
                                lnB[:, o + 3:o + 4], lnB[:, o + 1:o + 2],
                                1.0 / DM, None, ALU.mult))
                            v_Vs = P.inc("vector", "V", lambda e: e.nop())
                            P.wait("vector", "V", v_Vs)
                            P.run("vector", lambda e, o=o: e.tensor_tensor(
                                lnB[:, o + 4:o + 5], lnB[:, o + 2:o + 3],
                                lnB[:, o + 2:o + 3], ALU.mult))
                            v_Vs = P.inc("vector", "V", lambda e: e.nop())
                            P.wait("vector", "V", v_Vs)
                            P.run("vector", lambda e, o=o: e.tensor_tensor(
                                lnB[:, o + 5:o + 6], lnB[:, o + 3:o + 4],
                                lnB[:, o + 4:o + 5], ALU.subtract))
                            v_Vs = P.inc("vector", "V", lambda e: e.nop())
                            P.wait("vector", "V", v_Vs)
                            v_Vvar = P.inc("vector", "V", lambda e, o=o: e.tensor_scalar(
                                lnB[:, o + 5:o + 6], lnB[:, o + 5:o + 6],
                                EPS, None, ALU.add))
                            P.wait("scalar", "V", v_Vvar)
                            vAx = P.inc("scalar", "A", lambda e, o=o: e.activation(
                                lnB[:, o + 6:o + 7], lnB[:, o + 5:o + 6],
                                AF.Ln))
                            P.wait("scalar", "A", vAx)
                            v_Ainv = P.inc("scalar", "A", lambda e, o=o: e.activation(
                                lnB[:, o + 7:o + 8], lnB[:, o + 6:o + 7],
                                AF.Exp, scale=c_k[:, 1:2]))
                            P.wait("vector", "A", v_Ainv)
                            P.run("vector", lambda e, o=o: e.tensor_tensor(
                                lnB[:, o + 8:o + 9], lnB[:, o + 2:o + 3],
                                lnB[:, o + 7:o + 8], ALU.mult))
                            v_Vs = P.inc("vector", "V", lambda e: e.nop())
                            P.wait("vector", "V", v_Vs)
                            v_Vnb = P.inc("vector", "V", lambda e, o=o: e.tensor_scalar(
                                lnB[:, o + 9:o + 10], lnB[:, o + 8:o + 9],
                                -1.0, None, ALU.mult))
                            P.wait("scalar", "V", v_Vnb)
                            P.wait("scalar", "A", v_Ainv)
                            if glt >= 2:
                                P.wait("scalar", f"os{lb}", v_out_st[glt - 2])
                            v_out = P.inc("scalar", "A", lambda e, lb=lb, o=o: e.activation(
                                ot[lb][:, :], sq2[:, :], AF.Identity,
                                bias=lnB[:, o + 9:o + 10], scale=lnB[:, o + 7:o + 8]))
                            v_ln2_done[glt % 8 if half == 0 else glt] = v_out
                            v_ln2_done[glt] = v_out
                            P.wait("sync", "A", v_out)
                            v_out_st[glt] = P.inc(
                                "sync", f"os{lb}",
                                lambda e, tok0=tok0, lb=lb: e.dma_start(
                                    out=out[tok0:tok0 + 128, :], in_=ot[lb][:, :]), 16)

                    P.wait("sync", "os0", P.val("os0"))
                    P.wait("sync", "os1", P.val("os1"))

        except _StopEmit:
            pass

        with nc.Block() as block:
            for eng in _Plan.ENGINES:
                def _mk(eng):
                    def _run(e):
                        for f in P.ops[eng]:
                            f(e)
                    return _run
                getattr(block, eng)(_mk(eng))

    return nc


def _prep_inputs(inputs):
    x = np.asarray(inputs["L0_attn_x"])
    ipw = np.asarray(inputs["L0_attn_in_proj_w"])
    cw = np.asarray(inputs["L0_attn_conv_w"])
    cb = np.asarray(inputs["L0_attn_conv_b"])
    xpw = np.asarray(inputs["L0_attn_x_proj_w"])
    dtw = np.asarray(inputs["L0_attn_dt_w"])
    dtb = np.asarray(inputs["L0_attn_dt_b"])
    alog = np.asarray(inputs["L0_attn_A_log"])
    Dp = np.asarray(inputs["L0_attn_D"])
    wout = np.asarray(inputs["L0_attn_out_proj_w"])
    Wr = np.asarray(inputs["L0_ffn_Wr"])
    br = np.asarray(inputs["L0_ffn_br"])
    W1 = np.asarray(inputs["L0_ffn_W1"])
    b1 = np.asarray(inputs["L0_ffn_b1"])
    W2 = np.asarray(inputs["L0_ffn_W2"])
    b2 = np.asarray(inputs["L0_ffn_b2"])

    xT = np.ascontiguousarray(x[0].T).astype(np.float32)
    p = np.arange(128)
    g_exp = np.zeros((128, 16, 128), np.float32)
    m_idx = 8 * np.arange(16)[None, :] + (p % 8)[:, None]
    g_exp[p[:, None], np.arange(16)[None, :], m_idx] = 1.0
    g_exp = np.ascontiguousarray(g_exp.reshape(128, 16 * 128))
    wr_pat = np.zeros((128, 64), np.float32)
    for kt in range(8):
        wr_pat[:, kt * 8:(kt + 1) * 8] = Wr[kt * 128:(kt + 1) * 128, :]
    b1_pat = np.zeros((128, 8 * 16), np.float32)
    for e in range(8):
        for j in range(16):
            b1_pat[:, e * 16 + j] = b1[e, j * 128 + p]
    ident = np.eye(128, dtype=np.float32)
    ones = np.ones((1, 128), np.float32)

    maps = []
    for c in range(NCORES):
        d0 = c * DC
        dd = d0 + 8 * np.arange(NT)[None, :] + (p % 8)[:, None]
        xpw_p = np.zeros((128, 66), np.float32)
        xpw_p[:, 0:33] = xpw[d0 + p, :]
        xpw_p[:, 33:66] = xpw[d0 + 128 + p, :]
        convw_p = np.zeros((128, 8), np.float32)
        convb_p = np.zeros((128, 2), np.float32)
        d_p = np.zeros((128, 2), np.float32)
        dtw_s = np.zeros((128, 2), np.float32)
        dtb_s = np.zeros((128, 2), np.float32)
        for pt in range(2):
            rows = d0 + pt * 128 + p
            convw_p[:, pt * 4:(pt + 1) * 4] = cw[rows, 0, :]
            convb_p[:, pt] = cb[rows]
            d_p[:, pt] = Dp[rows]
            dtw_s[:, pt] = dtw[rows]
            dtb_s[:, pt] = dtb[rows]
        maps.append({
            "xT": xT,
            "x_res": np.ascontiguousarray(x[0, c * TC:(c + 1) * TC, :]),
            "winc": np.ascontiguousarray(np.concatenate(
                [ipw[:, d0:d0 + DC], ipw[:, DI + d0:DI + d0 + DC]], axis=1)),
            "xpw_pat": xpw_p,
            "dtw_pat": np.ascontiguousarray(dtw[dd]).astype(np.float32),
            "dtb_pat": np.ascontiguousarray(dtb[dd]).astype(np.float32),
            "dtw_sm": dtw_s,
            "dtb_sm": dtb_s,
            "alog_pat": np.ascontiguousarray(
                alog[dd, (p // 8)[:, None]]).astype(np.float32),
            "d_pat": d_p,
            "convw_pat": convw_p,
            "convb_pat": convb_p,
            "g_exp": g_exp,
            "ident_in": ident,
            "ones_in": ones,
            "wout_in": wout,
            "wr_pat": wr_pat,
            "br_in": br.reshape(1, 8).astype(np.float32),
            "b1_pat": b1_pat,
            "b2_in": np.ascontiguousarray(b2.reshape(1, 8 * DM)),
            "w1c": np.ascontiguousarray(W1[c]),
            "w2c": np.ascontiguousarray(W2[c]),
        })
    return maps


def kernel(**inputs):
    maps = _prep_inputs(inputs)
    nc = build_program()
    res = run_bass_kernel_spmd(nc, maps, core_ids=list(range(NCORES)))
    full = np.concatenate([res.results[c]["out"] for c in range(NCORES)], axis=0)
    return full.reshape(1, L, DM).astype(np.float32)
